# version 06
# baseline (speedup 1.0000x reference)
import numpy as np
import concourse.bass as bass
import concourse.mybir as mybir
from concourse.bass_utils import run_bass_kernel_spmd

F32 = mybir.dt.float32
BF16 = mybir.dt.bfloat16
ALU = mybir.AluOpType
ACT = mybir.ActivationFunctionType
AX = mybir.AxisListType

T = 2048
TT = 16
D = 1024
DC = 8
H = 16
DK = 64
E = 16
I = 512
ICn = 4
PG = 256
NG = 8
CPG = 2
NCH = 16
EPS_LN = 1e-5
EPS_Z = 1e-6
DELTA = 1e-30
BIG = 1e30

PAIR_AT = True
PAIR_NUM = False
ZERO_NUM = False
NB_LAST = {6: 0, 13: 1, 15: 2}
NSL = [7, 7, 2]


def nb_slot(h):
    b = 0 if h < 7 else (1 if h < 14 else 2)
    return b, h - 7 * b


def s_bank_slot(h):
    j = h // 2
    return (0, j) if j < 7 else (1, j - 7)


def build():
    nc = bass.Bass("TRN2", target_bir_lowering=False)

    x_d = nc.dram_tensor("x", [T, D], F32, kind="ExternalInput").ap()
    wqT_d = nc.dram_tensor("wqT", [D, D], F32, kind="ExternalInput").ap()
    wkT_d = nc.dram_tensor("wkT", [D, D], F32, kind="ExternalInput").ap()
    wvT_d = nc.dram_tensor("wvT", [D, D], F32, kind="ExternalInput").ap()
    woT_d = nc.dram_tensor("woT", [D, D], F32, kind="ExternalInput").ap()
    bq_d = nc.dram_tensor("bq", [128, DC], F32, kind="ExternalInput").ap()
    bk_d = nc.dram_tensor("bk", [128, DC], F32, kind="ExternalInput").ap()
    bv_d = nc.dram_tensor("bv", [D], F32, kind="ExternalInput").ap()
    bo_d = nc.dram_tensor("bo", [D], F32, kind="ExternalInput").ap()
    gwT_d = nc.dram_tensor("gwT", [D, E], F32, kind="ExternalInput").ap()
    eb_d = nc.dram_tensor("eb", [E], F32, kind="ExternalInput").ap()
    gpT_d = nc.dram_tensor("gpT", [E, D, I], F32, kind="ExternalInput").ap()
    upT_d = nc.dram_tensor("upT", [E, D, I], F32, kind="ExternalInput").ap()
    dpT_d = nc.dram_tensor("dpT", [E, I, D], F32, kind="ExternalInput").ap()
    out_d = nc.dram_tensor("out", [T, D], F32, kind="ExternalOutput").ap()

    x_sb = nc.alloc_sbuf_tensor("x_sb", [128, TT, D], F32)
    lnsum = nc.alloc_sbuf_tensor("lnsum", [128, TT], F32)
    lnsq = nc.alloc_sbuf_tensor("lnsq", [128, TT], F32)
    lnt1 = nc.alloc_sbuf_tensor("lnt1", [128, TT], F32)
    lnt2 = nc.alloc_sbuf_tensor("lnt2", [128, TT], F32)
    ln1m = nc.alloc_sbuf_tensor("ln1m", [128, TT], F32)
    ln1r = nc.alloc_sbuf_tensor("ln1r", [128, TT], F32)
    ln2m = nc.alloc_sbuf_tensor("ln2m", [128, TT], F32)
    ln2r = nc.alloc_sbuf_tensor("ln2r", [128, TT], F32)
    ones128 = nc.alloc_sbuf_tensor("ones128", [1, 128], BF16)
    zrow = nc.alloc_sbuf_tensor("zrow", [1, 504], BF16)
    onesq = nc.alloc_sbuf_tensor("onesq", [128, 128], BF16)
    mask_sb = nc.alloc_sbuf_tensor("mask_sb", [128, 128], BF16)
    junk1 = nc.alloc_sbuf_tensor("junk1", [128, 16], F32)
    junk2 = nc.alloc_sbuf_tensor("junk2", [128, 16], F32)
    bqf = nc.alloc_sbuf_tensor("bqf", [128, DC], F32)
    bkf = nc.alloc_sbuf_tensor("bkf", [128, DC], F32)
    bvrow = nc.alloc_sbuf_tensor("bvrow", [1, D], BF16)
    borow = nc.alloc_sbuf_tensor("borow", [1, D], BF16)
    gwf = nc.alloc_sbuf_tensor("gwf", [128, DC, E], F32)
    gwhi = nc.alloc_sbuf_tensor("gwhi", [128, DC, E], BF16)
    gwlo = nc.alloc_sbuf_tensor("gwlo", [128, DC, E], BF16)
    ebf = nc.alloc_sbuf_tensor("ebf", [1, E], F32)
    ebhi = nc.alloc_sbuf_tensor("ebhi", [1, E], BF16)
    eblo = nc.alloc_sbuf_tensor("eblo", [1, E], BF16)
    ebb = nc.alloc_sbuf_tensor("ebb", [128, E], F32)
    S_sb = nc.alloc_sbuf_tensor("S_sb", [128, 2, 504], BF16)
    zt = nc.alloc_sbuf_tensor("zt", [128, H], F32)
    zrcp = nc.alloc_sbuf_tensor("zrcp", [128, H], F32)
    wg0ded = nc.alloc_sbuf_tensor("wg0ded", [128, DC, I], BF16)
    wu0ded = nc.alloc_sbuf_tensor("wu0ded", [128, DC, I], BF16)

    AR = 113 * 1024
    ar0, _ = nc.bump_sbuf(AR)

    def at(name, shape, dtype, off):
        return nc.alloc_sbuf_tensor_at(name, shape, dtype, offset=ar0 + off)

    wqT_sb = at("wqT_sb", [128, DC, D], BF16, 0)
    wkT_sb = at("wkT_sb", [128, DC, D], BF16, 16384)
    wvT_sb = at("wvT_sb", [128, DC, D], BF16, 32768)
    woT_sb = at("woT_sb", [128, DC, D], BF16, 49152)
    h1_tm = at("h1_tm", [128, CPG, D], BF16, 65536)
    h1T_g = at("h1T_g", [128, DC, PG], BF16, 69632)
    qT_g = at("qT_g", [128, DC, PG], BF16, 73728)
    kT_g = at("kT_g", [128, DC, PG], BF16, 77824)
    ktm_g = at("ktm_g", [128, CPG, D], BF16, 81920)
    vp_g = at("vp_g", [128, CPG, H * 65], BF16, 86016)
    ATm_sb = at("ATm_sb", [128, H, 128], BF16, 90176)
    attn_tm = at("attn_tm", [128, CPG, D], BF16, 94272)
    attnT_g = at("attnT_g", [128, DC, PG], BF16, 98368)
    phis_e = at("phis_e", [128, 2, PG], F32, 102464)
    phis_m = at("phis_m", [128, 2, PG], F32, 104512)
    sqjunk = at("sqjunk", [128, D], BF16, 106560)

    h2T_sb = at("h2T_sb", [128, DC, T], BF16, 0)
    actT = at("actT", [128, ICn, T], BF16, 32768)
    silu_sb = at("silu_sb", [128, 2, 512], F32, 49152)
    R_sb = at("R_sb", [128, 256], F32, 53248)
    SC_sb = at("SC_sb", [128, 256], F32, 54272)
    SFC = at("SFC", [128, 256], F32, 55296)
    TMPR = at("TMPR", [128, 256], F32, 56320)
    SELR = at("SELR", [128, 256], F32, 57344)
    WR = at("WR", [128, 256], F32, 58368)
    CMB = at("CMB", [128, 256], F32, 59392)
    rpool = at("rpool", [128, 32, 16], F32, 60416)
    EQR = at("EQR", [128, 256], F32, 62464)
    h2f = at("h2f", [128, CPG, D], F32, 63488)
    h2hi_tm = at("h2hi_tm", [128, CPG, D], BF16, 71680)
    h2lo_tm = at("h2lo_tm", [128, CPG, D], BF16, 75776)
    h2loT = at("h2loT", [128, DC, T], BF16, 79872)
    wg_arena = at("wg1a", [128, DC, I], BF16, 63488 + 24576)
    wu_arena = at("wu1a", [128, DC, I], BF16, 71680 + 24576)
    wd_sb = [at(f"wd{p}", [128, ICn, D], BF16, 79872 + p * 24576) for p in range(2)]
    wg_sb = [wg0ded, wg_arena]
    wu_sb = [wu0ded, wu_arena]

    pb = [nc.alloc_psum_tensor(f"pb{i}", [128, 512], F32) for i in range(2)]
    numb = [nc.alloc_psum_tensor(f"numb{i}", [128, 512], F32) for i in range(3)]
    Sps = [nc.alloc_psum_tensor(f"Sps{i}", [128, 512], F32) for i in range(2)]
    pbc = nc.alloc_psum_tensor("pbc", [128, 512], F32)

    sem_cm = [nc.semaphore(n) for n in ("ld", "sy", "mm", "vx", "sc", "st")]
    sem_ld, sem_sy, sem_mm, sem_vx, sem_sc, sem_st = [c.__enter__() for c in sem_cm]
    sems = {"ld": sem_ld, "sy": sem_sy, "mm": sem_mm, "vx": sem_vx,
            "sc": sem_sc, "st": sem_st}
    cnt = {k: 0 for k in sems}
    ops = {k: [] for k in ("g", "y", "t", "v", "s")}

    def emit(eng, fn):
        ops[eng].append(fn)

    def inc(name):
        cnt[name] += 16 if name in ("ld", "sy", "st") else 1
        return cnt[name]

    def wt(eng, name, val):
        if val:
            s = sems[name]
            emit(eng, lambda e, s=s, v=val: e.wait_ge(s, v))

    def wt2(eng, ev):
        if ev is not None:
            wt(eng, ev[0], ev[1])

    def spacer(e):
        e.tensor_copy(junk2[:, 0:8], junk1[:, 0:8])

    rx = x_d.rearrange("(tt p) d -> p tt d", p=128)
    xld = []
    for a in range(4):
        def f(e, a=a):
            e.dma_start(x_sb[:, 4 * a:4 * a + 4, :], rx[:, 4 * a:4 * a + 4, :]).then_inc(sem_sy, 16)
        emit("y", f)
        xld.append(inc("sy"))

    def f(e):
        e.dma_start(bqf[:, :], bq_d).then_inc(sem_sy, 16)
        e.dma_start(bkf[:, :], bk_d).then_inc(sem_sy, 16)
        e.dma_start(gwf[:, :, :], gwT_d.rearrange("(c p) e -> p c e", p=128)).then_inc(sem_sy, 16)
        e.dma_start(ebf[0:1, :], eb_d).then_inc(sem_sy, 16)
    emit("y", f)
    for _ in range(4):
        inc("sy")
    smallld = cnt["sy"]

    wld = {}
    for key, w_sb, dd in (("wqT_sb", wqT_sb, wqT_d), ("wkT_sb", wkT_sb, wkT_d),
                          ("wvT_sb", wvT_sb, wvT_d), ("woT_sb", woT_sb, woT_d)):
        def f(e, w_sb=w_sb, dd=dd):
            e.dma_start(w_sb[:, :, :], dd.rearrange("(c p) o -> p c o", p=128)).then_inc(sem_ld, 16)
        emit("g", f)
        wld[key] = inc("ld")

    def f(e):
        e.dma_start(bvrow[0:1, :], bv_d).then_inc(sem_ld, 16)
        e.dma_start(borow[0:1, :], bo_d).then_inc(sem_ld, 16)
    emit("g", f)
    inc("ld")
    rowld = inc("ld")

    def f(e):
        e.memset(ones128[0:1, :], 1.0)
        e.memset(zrow[0:1, :], 0.0)
        e.memset(onesq[:, :], 1.0)
        e.memset(junk1[:, :], 0.0)
        e.memset(vp_g[:, :, :].rearrange("p c (h z) -> p c h z", z=65)[:, :, :, 64:65], 1.0)
        e.affine_select(mask_sb[:, :], onesq[:, :], pattern=[[1, 128]],
                        compare_op=ALU.is_ge, fill=0.0, base=0,
                        channel_multiplier=-1).then_inc(sem_ld, 16)
    emit("g", f)
    consts_done = inc("ld")

    def f(e):
        e.dma_start(wg0ded[:, :, :], gpT_d[0].rearrange("(c p) i -> p c i", p=128)).then_inc(sem_ld, 16)
        e.dma_start(wu0ded[:, :, :], upT_d[0].rearrange("(c p) i -> p c i", p=128)).then_inc(sem_ld, 16)
    emit("g", f)
    inc("ld")
    e0gu_ld = inc("ld")

    def f(e):
        e.memset(S_sb[:, :, :], 0.0)
    emit("v", f)
    wt("v", "sy", smallld)

    def f(e):
        e.tensor_copy(ebhi[0:1, :], ebf[0:1, :])
        e.tensor_copy(gwhi[:, :, :], gwf[:, :, :])
        e.tensor_sub(eblo[0:1, :], ebf[0:1, :], ebhi[0:1, :])
        e.tensor_sub(gwlo[:, :, :], gwf[:, :, :], gwhi[:, :, :]).then_inc(sem_vx, 1)
    emit("v", f)
    hilo_done = inc("vx")

    wt("t", "vx", hilo_done)
    wt("t", "ld", consts_done)

    def f(e):
        e.matmul(pbc[:, 0:E], ones128[0:1, :], ebhi[0:1, :], start=True, stop=False)
        e.matmul(pbc[:, 0:E], ones128[0:1, :], eblo[0:1, :], start=False, stop=True).then_inc(sem_mm, 1)
    emit("t", f)
    ebb_mm = inc("mm")
    wt("v", "mm", ebb_mm)

    def f(e):
        e.tensor_copy(ebb[:, :], pbc[:, 0:E]).then_inc(sem_vx, 1)
    emit("v", f)
    ebb_done = inc("vx")

    red_done = None
    for i in range(TT):
        wt("v", "sy", xld[i // 4])
        if i == TT - 1:
            def f(e, i=i):
                e.tensor_reduce(lnsum[:, i:i + 1], x_sb[:, i, :], axis=AX.X,
                                op=ALU.add).then_inc(sem_vx, 1)
            emit("v", f)
            red_done = inc("vx")
        else:
            def f(e, i=i):
                e.tensor_reduce(lnsum[:, i:i + 1], x_sb[:, i, :], axis=AX.X, op=ALU.add)
            emit("v", f)
        wt("s", "sy", xld[i // 4])
        def f(e, i=i):
            e.activation(sqjunk[:, :], x_sb[:, i, :], ACT.Square, bias=0.0, scale=1.0,
                         accum_out=lnsq[:, i:i + 1]).then_inc(sem_sc, 1)
        emit("s", f)
        inc("sc")
    sq1_done = cnt["sc"]
    wt("v", "vx", red_done)
    wt("v", "sc", sq1_done)

    def f(e):
        e.tensor_scalar_mul(ln1m[:, :], lnsum[:, :], 1.0 / D)
        e.tensor_scalar_mul(lnt1[:, :], lnsq[:, :], 1.0 / D)
        e.tensor_mul(lnt2[:, :], ln1m[:, :], ln1m[:, :])
        spacer(e)
        e.tensor_sub(lnt1[:, :], lnt1[:, :], lnt2[:, :])
        spacer(e)
        e.tensor_scalar_add(lnt1[:, :], lnt1[:, :], EPS_LN).then_inc(sem_vx, 1)
    emit("v", f)
    var1_done = inc("vx")
    wt("s", "vx", var1_done)

    def f(e):
        e.activation(lnt2[:, :], lnt1[:, :], ACT.Sqrt, bias=0.0, scale=1.0).then_inc(sem_sc, 1)
    emit("s", f)
    sqrt1_done = inc("sc")
    wt("v", "sc", sqrt1_done)

    def f(e):
        e.reciprocal(ln1r[:, :], lnt2[:, :])
        spacer(e)
        e.tensor_copy(junk2[:, 0:8], junk1[:, 0:8]).then_inc(sem_vx, 1)
    emit("v", f)
    ln1_ready = inc("vx")

    pb_free = [None, None]
    pb_mm = [0, 0]
    num_free = [None, None, None]
    phis_free = [0, 0]
    prev_S_evac = 0
    S_sb_ready = 0
    gbank = [0]
    grp_state = {}

    def emit_apply_h1T(g):
        tiles = [2 * g, 2 * g + 1]
        apply_done = []
        wt("v", "vx", ln1_ready)
        for jj, ti in enumerate(tiles):
            def f(e, jj=jj, ti=ti):
                e.tensor_scalar(h1_tm[:, jj, :], x_sb[:, ti, :],
                                ln1m[:, ti:ti + 1], ln1r[:, ti:ti + 1],
                                op0=ALU.subtract, op1=ALU.mult).then_inc(sem_vx, 1)
            emit("v", f)
            apply_done.append(inc("vx"))
        h1T_done = 0
        for jj in range(CPG):
            wt("y", "vx", apply_done[jj])
            def f(e, jj=jj):
                e.dma_start(h1T_g[:, :, jj * 128:(jj + 1) * 128], h1_tm[:, jj, :],
                            transpose=True).then_inc(sem_sy, 16)
            emit("y", f)
            h1T_done = inc("sy")
        grp_state[g] = {"h1T": h1T_done}

    def emit_proj(g):
        st_ = grp_state[g]
        kevac = [0] * DC

        def proj_fm(w_sb, bias_t, dst, wld_val, is_k):
            for oc in range(DC):
                b = gbank[0] % 2
                gbank[0] += 1
                wt("t", "sy", st_["h1T"])
                wt("t", "ld", wld_val)
                wt2("t", pb_free[b])

                def f(e, b=b, w_sb=w_sb, oc=oc):
                    for dc in range(DC):
                        mmi = e.matmul(pb[b][:, 0:PG], w_sb[:, dc, oc * 128:(oc + 1) * 128],
                                       h1T_g[:, dc, :], start=(dc == 0), stop=(dc == DC - 1))
                        if dc == DC - 1:
                            mmi.then_inc(sem_mm, 1)
                emit("t", f)
                pb_mm[b] = inc("mm")
                pp = oc % 2
                wt("s", "mm", pb_mm[b])
                wt("s", "vx", phis_free[pp])

                def f(e, b=b, oc=oc, pp=pp, bias_t=bias_t):
                    e.activation(phis_e[:, pp, :], pb[b][:, 0:PG], ACT.Exp,
                                 bias=bias_t[:, oc:oc + 1], scale=1.0)
                    e.activation(phis_m[:, pp, :], pb[b][:, 0:PG], ACT.Relu,
                                 bias=bias_t[:, oc:oc + 1], scale=1.0).then_inc(sem_sc, 1)
                emit("s", f)
                sc_done = inc("sc")
                pb_free[b] = ("sc", sc_done)
                wt("v", "sc", sc_done)

                def f(e, pp=pp, oc=oc, dst=dst):
                    e.scalar_tensor_tensor(dst[:, oc, :], phis_e[:, pp, :], 1.0,
                                           phis_m[:, pp, :], op0=ALU.min,
                                           op1=ALU.add).then_inc(sem_vx, 1)
                emit("v", f)
                d = inc("vx")
                phis_free[pp] = d
                if is_k:
                    kevac[oc] = d

        proj_fm(wqT_sb, bqf, qT_g, wld["wqT_sb"], False)
        proj_fm(wkT_sb, bkf, kT_g, wld["wkT_sb"], True)

        for jj in range(CPG):
            for n in range(2):
                b = gbank[0] % 2
                gbank[0] += 1
                wt("t", "ld", max(wld["wvT_sb"], rowld))
                wt2("t", pb_free[b])

                def f(e, b=b, jj=jj, n=n):
                    for dc in range(DC):
                        e.matmul(pb[b][:, 0:512], h1T_g[:, dc, jj * 128:(jj + 1) * 128],
                                 wvT_sb[:, dc, n * 512:(n + 1) * 512],
                                 start=(dc == 0), stop=False)
                    e.matmul(pb[b][:, 0:512], ones128[0:1, :],
                             bvrow[0:1, n * 512:(n + 1) * 512],
                             start=False, stop=True).then_inc(sem_mm, 1)
                emit("t", f)
                pb_mm[b] = inc("mm")
                wt("v", "mm", pb_mm[b])

                def f(e, b=b, jj=jj, n=n):
                    dstv = vp_g[:, jj, :].rearrange("p (h z) -> p h z", z=65)[:, 8 * n:8 * n + 8, 0:64]
                    e.tensor_copy(dstv, pb[b][:, 0:512]).then_inc(sem_vx, 1)
                emit("v", f)
                pb_free[b] = ("vx", inc("vx"))
        st_["vevac"] = cnt["vx"]
        st_["kevac"] = kevac

    def emit_ktm(g):
        st_ = grp_state[g]
        ktm_done = 0
        for dc in range(DC):
            wt("y", "vx", st_["kevac"][dc])
            def f(e, dc=dc):
                e.dma_start(ktm_g[:, :, dc * 128:(dc + 1) * 128], kT_g[:, dc, :],
                            transpose=True).then_inc(sem_sy, 16)
            emit("y", f)
            ktm_done = inc("sy")
        st_["ktm"] = ktm_done

    def emit_inner_chunk(g, cc):
        nonlocal prev_S_evac, S_sb_ready
        st_ = grp_state[g]
        c = 2 * g + cc
        wt("t", "sy", st_["ktm"])
        wt("t", "vx", st_["vevac"])
        wt("t", "vx", S_sb_ready)
        for b in range(3):
            wt2("t", num_free[b])
        if ZERO_NUM:
            for b in range(3):
                def f(e, b=b):
                    e.matmul(numb[b][:, 0:504], ones128[0:1, :], zrow[0:1, 0:504],
                             start=True, stop=False)
                emit("t", f)
        if c == 0:
            for b in range(2):
                def f(e, b=b):
                    e.matmul(Sps[b][:, 0:504], ones128[0:1, :], zrow[0:1, 0:504],
                             start=True, stop=False)
                emit("t", f)
        wt("t", "vx", prev_S_evac)

        at_mm = [0] * H
        atm_done = [0] * H
        for hp in range(H // 2):
            h0, h1 = 2 * hp, 2 * hp + 1
            wt2("t", pb_free[0])
            wt2("t", pb_free[1])

            if PAIR_AT:
                def f(e, hp=hp, cc=cc):
                    hc = hp
                    e.matmul(pb[0][:, 0:128],
                             kT_g[0:64, hc, cc * 128:(cc + 1) * 128],
                             qT_g[0:64, hc, cc * 128:(cc + 1) * 128],
                             start=True, stop=True).then_inc(sem_mm, 1)
                    e.matmul(pb[1][:, 0:128],
                             kT_g[64:128, hc, cc * 128:(cc + 1) * 128],
                             qT_g[64:128, hc, cc * 128:(cc + 1) * 128],
                             start=True, stop=True).then_inc(sem_mm, 1)
                emit("t", f)
            else:
                for h, b in ((h0, 0), (h1, 1)):
                    po = (h % 2) * 64

                    def f(e, b=b, po=po, hp=hp, cc=cc):
                        e.matmul(pb[b][:, 0:128],
                                 kT_g[po:po + 64, hp, cc * 128:(cc + 1) * 128],
                                 qT_g[po:po + 64, hp, cc * 128:(cc + 1) * 128],
                                 start=True, stop=True).then_inc(sem_mm, 1)
                    emit("t", f)
            at_mm[h0] = inc("mm")
            at_mm[h1] = inc("mm")
            for h, b in ((h0, 0), (h1, 1)):
                wt("v", "mm", at_mm[h])

                def f(e, b=b, h=h):
                    e.tensor_mul(ATm_sb[:, h, :], pb[b][:, 0:128], mask_sb[:, :]).then_inc(sem_vx, 1)
                emit("v", f)
                atm_done[h] = inc("vx")
                pb_free[b] = ("vx", atm_done[h])

        num_mm = [0, 0, 0]
        s_mm_last = 0
        for hp in range(H // 2):
            h0, h1 = 2 * hp, 2 * hp + 1
            wt("t", "vx", atm_done[h1])

            def f(e, hp=hp, cc=cc, c=c):
                def intra(h):
                    nb, sl = nb_slot(h)
                    e.matmul(numb[nb][:, sl * 72:sl * 72 + 65], ATm_sb[:, h, :],
                             vp_g[:, cc, h * 65:(h + 1) * 65],
                             start=((not ZERO_NUM) and h in (0, 7, 14)), stop=False)

                def inter(h):
                    nb, sl = nb_slot(h)
                    sb_, sj = s_bank_slot(h)
                    po = (h % 2) * 64
                    e.matmul(numb[nb][:, sl * 72:sl * 72 + 65],
                             qT_g[po:po + 64, hp, cc * 128:(cc + 1) * 128],
                             S_sb[po:po + 64, sb_, sj * 72:sj * 72 + 65],
                             start=False, stop=(h in NB_LAST)).then_inc(sem_mm, 1)

                def supd(h):
                    sb_, sj = s_bank_slot(h)
                    po = (h % 2) * 64
                    e.matmul(Sps[sb_][po:po + 64, sj * 72:sj * 72 + 65],
                             ktm_g[:, cc, h * 64:(h + 1) * 64],
                             vp_g[:, cc, h * 65:(h + 1) * 65],
                             start=False, stop=(c == NCH - 1 and h in (13, 15)),
                             tile_position=(0, po) if po else None).then_inc(sem_mm, 1)

                h0_, h1_ = 2 * hp, 2 * hp + 1
                if PAIR_NUM:
                    intra(h0_); inter(h0_); intra(h1_); inter(h1_); supd(h0_); supd(h1_)
                else:
                    intra(h0_); inter(h0_); intra(h1_); inter(h1_)
                    supd(h0_); supd(h1_)
            emit("t", f)
            for h in (h0, h1):
                inc("mm")
            for h in (h0, h1):
                m2 = inc("mm")
            for h in (h0, h1):
                if h in NB_LAST:
                    num_mm[NB_LAST[h]] = cnt["mm"]
            s_mm_last = cnt["mm"]

        for b in range(3):
            wt("v", "mm", num_mm[b])

        def f(e):
            for b in range(3):
                src = numb[b][:, 0:504].rearrange("p (k r) -> p k r", r=72)[:, 0:NSL[b], 64:65]
                e.tensor_copy(zt[:, 7 * b:7 * b + NSL[b]], src)
            spacer(e)
            e.tensor_scalar_add(zt[:, :], zt[:, :], EPS_Z)
        emit("v", f)
        wt("v", "mm", s_mm_last)

        def f(e):
            e.tensor_copy(S_sb[:, 0, :], Sps[0][:, 0:504])
            e.reciprocal(zrcp[:, :], zt[:, :])
            e.tensor_copy(S_sb[:, 1, :], Sps[1][:, 0:504]).then_inc(sem_vx, 1)
        emit("v", f)
        prev_S_evac = S_sb_ready = inc("vx")

        for b in range(3):
            nh = NSL[b]

            def f(e, b=b, nh=nh, cc=cc):
                src = numb[b][:, 0:504].rearrange("p (k r) -> p k r", r=72)[:, 0:nh, 0:64]
                rb = zrcp[:, 7 * b:7 * b + nh].rearrange("p (k o) -> p k o", o=1) \
                    .broadcast_to([128, nh, 64])
                dstv = attn_tm[:, cc, :].rearrange("p (h o) -> p h o", o=64)[:, 7 * b:7 * b + nh, :]
                e.tensor_mul(dstv, src, rb).then_inc(sem_vx, 1)
            emit("v", f)
            num_free[b] = ("vx", inc("vx"))
        st_[f"div{cc}"] = cnt["vx"]

        wt("y", "vx", st_[f"div{cc}"])

        def f(e, cc=cc):
            e.dma_start(attnT_g[:, :, cc * 128:(cc + 1) * 128], attn_tm[:, cc, :],
                        transpose=True).then_inc(sem_sy, 16)
        emit("y", f)
        st_[f"attnT{cc}"] = inc("sy")

    def emit_wo(g, cc):
        st_ = grp_state[g]
        ti = 2 * g + cc
        evs = []
        for n in range(2):
            b = gbank[0] % 2
            gbank[0] += 1
            wt("t", "sy", st_[f"attnT{cc}"])
            wt("t", "ld", max(wld["woT_sb"], rowld))
            wt2("t", pb_free[b])

            def f(e, b=b, cc=cc, n=n):
                for dc in range(DC):
                    e.matmul(pb[b][:, 0:512], attnT_g[:, dc, cc * 128:(cc + 1) * 128],
                             woT_sb[:, dc, n * 512:(n + 1) * 512],
                             start=(dc == 0), stop=False)
                e.matmul(pb[b][:, 0:512], ones128[0:1, :],
                         borow[0:1, n * 512:(n + 1) * 512],
                         start=False, stop=True).then_inc(sem_mm, 1)
            emit("t", f)
            pb_mm[b] = inc("mm")
            wt("v", "mm", pb_mm[b])

            def f(e, b=b, ti=ti, n=n):
                e.tensor_add(x_sb[:, ti, n * 512:(n + 1) * 512],
                             x_sb[:, ti, n * 512:(n + 1) * 512],
                             pb[b][:, 0:512]).then_inc(sem_vx, 1)
            emit("v", f)
            pb_free[b] = ("vx", inc("vx"))
            evs.append(cnt["vx"])
        return evs[-1]

    ln2_red_done = [0] * TT
    ln2_sq_done = [0] * TT

    def emit_ln2_tile(ti, gate_vx):
        wt("v", "vx", gate_vx)

        def f(e, ti=ti):
            e.tensor_reduce(lnsum[:, ti:ti + 1], x_sb[:, ti, :], axis=AX.X,
                            op=ALU.add).then_inc(sem_vx, 1)
        emit("v", f)
        ln2_red_done[ti] = inc("vx")
        wt("s", "vx", gate_vx)

        def f(e, ti=ti):
            e.activation(sqjunk[:, :], x_sb[:, ti, :], ACT.Square, bias=0.0, scale=1.0,
                         accum_out=lnsq[:, ti:ti + 1]).then_inc(sem_sc, 1)
        emit("s", f)
        ln2_sq_done[ti] = inc("sc")

    emit_apply_h1T(0)
    emit_proj(0)
    emit_ktm(0)
    for g in range(NG):
        if g + 1 < NG:
            emit_apply_h1T(g + 1)
        for cc in range(CPG):
            emit_inner_chunk(g, cc)
        if g + 1 < NG:
            emit_proj(g + 1)
            emit_ktm(g + 1)
        for cc in range(CPG):
            wo_ev = emit_wo(g, cc)
            emit_ln2_tile(2 * g + cc, wo_ev)
    wo_done = cnt["vx"]

    wt("v", "vx", ln2_red_done[TT - 1])
    wt("v", "sc", ln2_sq_done[TT - 1])

    def f(e):
        e.tensor_scalar_mul(ln2m[:, :], lnsum[:, :], 1.0 / D)
        e.tensor_scalar_mul(lnt1[:, :], lnsq[:, :], 1.0 / D)
        e.tensor_mul(lnt2[:, :], ln2m[:, :], ln2m[:, :])
        spacer(e)
        e.tensor_sub(lnt1[:, :], lnt1[:, :], lnt2[:, :])
        spacer(e)
        e.tensor_scalar_add(lnt1[:, :], lnt1[:, :], EPS_LN).then_inc(sem_vx, 1)
    emit("v", f)
    var2_done = inc("vx")
    wt("s", "vx", var2_done)

    def f(e):
        e.activation(lnt2[:, :], lnt1[:, :], ACT.Sqrt, bias=0.0, scale=1.0).then_inc(sem_sc, 1)
    emit("s", f)
    sqrt2_done = inc("sc")
    wt("v", "sc", sqrt2_done)

    def f(e):
        e.reciprocal(ln2r[:, :], lnt2[:, :])
        spacer(e)
        e.tensor_mul(lnt1[:, :], ln2m[:, :], ln2r[:, :])
        spacer(e)
        e.tensor_scalar_mul(lnt1[:, :], lnt1[:, :], -1.0)
        spacer(e)
        e.tensor_copy(junk2[:, 0:8], junk1[:, 0:8]).then_inc(sem_vx, 1)
    emit("v", f)
    ln2_ready = inc("vx")

    h2tr_done = [0] * TT
    for ti in range(TT):
        pp = ti % 2
        wt("v", "vx", ln2_ready)
        if ti >= 2:
            wt("v", "sy", h2tr_done[ti - 2])

        def f(e, ti=ti, pp=pp):
            e.tensor_scalar(h2f[:, pp, :], x_sb[:, ti, :],
                            ln2m[:, ti:ti + 1], ln2r[:, ti:ti + 1],
                            op0=ALU.subtract, op1=ALU.mult)
            e.tensor_copy(h2hi_tm[:, pp, :], h2f[:, pp, :])
            e.tensor_sub(h2lo_tm[:, pp, :], h2f[:, pp, :], h2hi_tm[:, pp, :]).then_inc(sem_vx, 1)
        emit("v", f)
        hd = inc("vx")
        wt("y", "vx", hd)

        def f(e, ti=ti, pp=pp):
            e.dma_start(h2T_sb[:, :, ti * 128:(ti + 1) * 128], h2hi_tm[:, pp, :],
                        transpose=True).then_inc(sem_sy, 16)
            e.dma_start(h2loT[:, :, ti * 128:(ti + 1) * 128], h2lo_tm[:, pp, :],
                        transpose=True).then_inc(sem_sy, 16)
        emit("y", f)
        inc("sy")
        h2tr_done[ti] = inc("sy")
    h2loT_done = h2tr_done[TT - 1]

    lg_banks = [pbc, numb[2]]
    lg_free = [num_free[2], None]
    lg_mm = [0, 0]
    GPb = [pb[0], numb[0]]
    UPb = [pb[1], numb[1]]
    gp_free_sc = [0, 0]
    up_free_vx = [0, 0]
    act_ready_vx = {}
    act_free_mm = 0

    def emit_gu(e_, pbuf, ic, q, wexp_gu):
        pp = (ic * 4 + q) % 2
        wt("t", "ld", wexp_gu)
        wt("t", "sc", gp_free_sc[pp])
        wt("t", "vx", up_free_vx[pp])

        def f(e, pbuf=pbuf, ic=ic, q=q, pp=pp):
            for dc in range(DC):
                mmi = e.matmul(GPb[pp][:, 0:512],
                               wg_sb[pbuf][:, dc, ic * 128:(ic + 1) * 128],
                               h2T_sb[:, dc, q * 512:(q + 1) * 512],
                               start=(dc == 0), stop=(dc == DC - 1))
                if dc == DC - 1:
                    mmi.then_inc(sem_mm, 1)
            for dc in range(DC):
                mmi = e.matmul(UPb[pp][:, 0:512],
                               wu_sb[pbuf][:, dc, ic * 128:(ic + 1) * 128],
                               h2T_sb[:, dc, q * 512:(q + 1) * 512],
                               start=(dc == 0), stop=(dc == DC - 1))
                if dc == DC - 1:
                    mmi.then_inc(sem_mm, 1)
        emit("t", f)
        gpmm = inc("mm")
        upmm = inc("mm")
        wt("s", "mm", gpmm)

        def f(e, pp=pp):
            e.activation(silu_sb[:, pp, :], GPb[pp][:, 0:512], ACT.Silu,
                         bias=0.0, scale=1.0).then_inc(sem_sc, 1)
        emit("s", f)
        gp_free_sc[pp] = inc("sc")
        wt("v", "mm", upmm)
        wt("v", "sc", gp_free_sc[pp])
        wt("v", "mm", act_free_mm)

        def f(e, ic=ic, q=q, pp=pp):
            e.tensor_mul(actT[:, ic, q * 512:(q + 1) * 512], UPb[pp][:, 0:512],
                         silu_sb[:, pp, :]).then_inc(sem_vx, 1)
        emit("v", f)
        up_free_vx[pp] = inc("vx")
        act_ready_vx[(e_, ic, q)] = up_free_vx[pp]

    for q in range(4):
        for tt in range(4 * q, 4 * q + 4):
            b = tt % 2
            wt("t", "sy", h2tr_done[tt])
            wt("t", "vx", hilo_done)
            wt2("t", lg_free[b])

            def f(e, b=b, tt=tt):
                pairs = [(h2T_sb, gwhi), (h2loT, gwhi), (h2T_sb, gwlo)]
                n_ = 0
                for (src, w) in pairs:
                    for dc in range(DC):
                        n_ += 1
                        mmi = e.matmul(lg_banks[b][:, 0:E], src[:, dc, tt * 128:(tt + 1) * 128],
                                       w[:, dc, :], start=(n_ == 1), stop=(n_ == 24))
                        if n_ == 24:
                            mmi.then_inc(sem_mm, 1)
            emit("t", f)
            lg_mm[b] = inc("mm")
            wt("v", "mm", lg_mm[b])

            def f(e, b=b, tt=tt):
                dst = R_sb[:, :].rearrange("p (z t) -> p z t", t=16)[:, :, tt:tt + 1]
                e.tensor_copy(dst, lg_banks[b][:, 0:E]).then_inc(sem_vx, 1)
            emit("v", f)
            lg_free[b] = ("vx", inc("vx"))
        for ic in range(ICn):
            emit_gu(0, 0, ic, q, e0gu_ld)
    Rcopy_done = cnt["vx"]
    logits_mm_done = max(lg_mm)

    wt("s", "vx", Rcopy_done)

    def f(e):
        e.activation(SC_sb[:, :], R_sb[:, :], ACT.Sigmoid, bias=0.0, scale=1.0).then_inc(sem_sc, 1)
    emit("s", f)
    sig_done = inc("sc")
    wt("v", "sc", sig_done)
    wt("v", "vx", ebb_done)

    def blk(tns, e_):
        return tns[:, e_ * 16:(e_ + 1) * 16]

    def f(e):
        for e_ in range(E):
            e.tensor_scalar_add(blk(SFC, e_), blk(SC_sb, e_), ebb[:, e_:e_ + 1])
        cmb2 = [(0, 1), (0, 2), (0, 3), (1, 2), (1, 3), (2, 3)]
        for gi in range(4):
            pr = [4 * gi + k for k in range(4)]
            for s_i, (a, b2) in enumerate(cmb2):
                e.tensor_add(rpool[:, 8 + s_i, :], blk(SFC, pr[a]), blk(SFC, pr[b2]))
            e.tensor_max(rpool[:, 14, :], rpool[:, 8, :], rpool[:, 9, :])
            e.tensor_max(rpool[:, 15, :], rpool[:, 10, :], rpool[:, 11, :])
            e.tensor_max(rpool[:, 16, :], rpool[:, 12, :], rpool[:, 13, :])
            e.tensor_max(rpool[:, 17, :], rpool[:, 14, :], rpool[:, 15, :])
            spacer(e)
            e.tensor_max(rpool[:, gi, :], rpool[:, 17, :], rpool[:, 16, :])
        for s_i, (a, b2) in enumerate(cmb2):
            e.tensor_tensor(rpool[:, 8 + s_i, :], rpool[:, a, :], rpool[:, b2, :], op=ALU.min)
        e.tensor_max(rpool[:, 14, :], rpool[:, 8, :], rpool[:, 9, :])
        e.tensor_max(rpool[:, 15, :], rpool[:, 10, :], rpool[:, 11, :])
        e.tensor_max(rpool[:, 16, :], rpool[:, 12, :], rpool[:, 13, :])
        e.tensor_max(rpool[:, 17, :], rpool[:, 14, :], rpool[:, 15, :])
        spacer(e)
        e.tensor_max(rpool[:, 6, :], rpool[:, 17, :], rpool[:, 16, :])
        spacer(e)
        spacer(e)
        for gi in range(4):
            e.tensor_tensor(rpool[:, 18 + gi, :], rpool[:, gi, :], rpool[:, 6, :], op=ALU.is_ge)
        for e_ in range(E):
            e.tensor_scalar_add(blk(TMPR, e_), blk(SFC, e_), e_ * DELTA)
        for e_ in range(E):
            e.tensor_mul(blk(TMPR, e_), blk(TMPR, e_), rpool[:, 18 + e_ // 4, :])
        for e_ in range(E):
            e.tensor_scalar_sub(blk(TMPR, e_), blk(TMPR, e_), e_ * DELTA)
        e.memset(SELR[:, :], 0.0)
        for r_ in range(4):
            for k in range(8):
                e.tensor_max(rpool[:, 8 + k, :], blk(TMPR, 2 * k), blk(TMPR, 2 * k + 1))
            for k in range(4):
                e.tensor_max(rpool[:, 16 + k, :], rpool[:, 8 + 2 * k, :], rpool[:, 9 + 2 * k, :])
            spacer(e)
            e.tensor_max(rpool[:, 20, :], rpool[:, 16, :], rpool[:, 17, :])
            e.tensor_max(rpool[:, 21, :], rpool[:, 18, :], rpool[:, 19, :])
            spacer(e)
            e.tensor_max(rpool[:, 22, :], rpool[:, 20, :], rpool[:, 21, :])
            spacer(e)
            spacer(e)
            for e_ in range(E):
                e.tensor_tensor(blk(EQR, e_), blk(TMPR, e_), rpool[:, 22, :], op=ALU.is_equal)
            for e_ in range(E):
                e.tensor_add(blk(SELR, e_), blk(SELR, e_), blk(EQR, e_))
            for e_ in range(E):
                e.scalar_tensor_tensor(blk(TMPR, e_), blk(EQR, e_), -BIG, blk(TMPR, e_),
                                       op0=ALU.mult, op1=ALU.add)
        for e_ in range(E):
            e.tensor_mul(blk(WR, e_), blk(SC_sb, e_), blk(SELR, e_))
        for k in range(8):
            e.tensor_add(rpool[:, 8 + k, :], blk(WR, 2 * k), blk(WR, 2 * k + 1))
        for k in range(4):
            e.tensor_add(rpool[:, 16 + k, :], rpool[:, 8 + 2 * k, :], rpool[:, 9 + 2 * k, :])
        spacer(e)
        e.tensor_add(rpool[:, 20, :], rpool[:, 16, :], rpool[:, 17, :])
        e.tensor_add(rpool[:, 21, :], rpool[:, 18, :], rpool[:, 19, :])
        spacer(e)
        e.tensor_add(rpool[:, 22, :], rpool[:, 20, :], rpool[:, 21, :])
        spacer(e)
        e.tensor_scalar_add(rpool[:, 22, :], rpool[:, 22, :], 1e-20)
        spacer(e)
        e.reciprocal(rpool[:, 23, :], rpool[:, 22, :])
        spacer(e)
        spacer(e)
        for e_ in range(E - 1):
            e.tensor_mul(blk(CMB, e_), blk(WR, e_), rpool[:, 23, :])
        e.tensor_mul(blk(CMB, E - 1), blk(WR, E - 1), rpool[:, 23, :]).then_inc(sem_vx, 1)
    emit("v", f)
    routing_done = inc("vx")

    DPb = [Sps[0], Sps[1]]
    dp_free_vx = [0, 0]
    down_last_mm = [0] * E

    for e_ in range(E):
        pbuf = e_ % 2
        if e_ == 0:
            wt("g", "mm", logits_mm_done)
            wt("g", "sy", h2loT_done)

            def f(e):
                e.dma_start(wd_sb[0][:, :, :],
                            dpT_d[0].rearrange("(c p) h -> p c h", p=128)).then_inc(sem_ld, 16)
            emit("g", f)
            wexp = inc("ld")
            wexp_gu = e0gu_ld
        else:
            if e_ >= 2:
                wt("g", "mm", down_last_mm[e_ - 2])
            wt("g", "mm", logits_mm_done)
            wt("g", "sy", h2loT_done)

            def f(e, e_=e_, pbuf=pbuf):
                e.dma_start(wg_sb[pbuf][:, :, :],
                            gpT_d[e_].rearrange("(c p) i -> p c i", p=128)).then_inc(sem_ld, 16)
                e.dma_start(wu_sb[pbuf][:, :, :],
                            upT_d[e_].rearrange("(c p) i -> p c i", p=128)).then_inc(sem_ld, 16)
                e.dma_start(wd_sb[pbuf][:, :, :],
                            dpT_d[e_].rearrange("(c p) h -> p c h", p=128)).then_inc(sem_ld, 16)
            emit("g", f)
            inc("ld")
            inc("ld")
            wexp = inc("ld")
            wexp_gu = wexp

        if e_ > 0:
            for ic in range(ICn):
                for q in range(4):
                    emit_gu(e_, pbuf, ic, q, wexp_gu)

        for j in range(TT):
            for n in range(2):
                pp = (j * 2 + n) % 2
                need = max(act_ready_vx[(e_, ic2, j // 4)] for ic2 in range(ICn))
                wt("t", "vx", need)
                wt("t", "ld", wexp)
                wt("t", "vx", dp_free_vx[pp])

                def f(e, pbuf=pbuf, j=j, n=n, pp=pp):
                    for ic in range(ICn):
                        mmi = e.matmul(DPb[pp][:, 0:512], actT[:, ic, j * 128:(j + 1) * 128],
                                       wd_sb[pbuf][:, ic, n * 512:(n + 1) * 512],
                                       start=(ic == 0), stop=(ic == ICn - 1))
                        if ic == ICn - 1:
                            mmi.then_inc(sem_mm, 1)
                emit("t", f)
                dpmm = inc("mm")
                down_last_mm[e_] = dpmm
                wt("v", "mm", dpmm)
                wt("v", "vx", routing_done)

                def f(e, e_=e_, j=j, n=n, pp=pp):
                    e.scalar_tensor_tensor(x_sb[:, j, n * 512:(n + 1) * 512],
                                           DPb[pp][:, 0:512],
                                           CMB[:, e_ * 16 + j:e_ * 16 + j + 1],
                                           x_sb[:, j, n * 512:(n + 1) * 512],
                                           op0=ALU.mult, op1=ALU.add).then_inc(sem_vx, 1)
                emit("v", f)
                dp_free_vx[pp] = inc("vx")
        act_free_mm = down_last_mm[e_]
    ffn_done = cnt["vx"]

    ro = out_d.rearrange("(tt p) d -> p tt d", p=128)
    for a in range(4):
        wt("y", "vx", ffn_done)

        def f(e, a=a):
            e.dma_start(ro[:, 4 * a:4 * a + 4, :], x_sb[:, 4 * a:4 * a + 4, :]).then_inc(sem_st, 16)
        emit("y", f)
        inc("st")
    wt("y", "st", cnt["st"])

    with nc.Block() as block:
        @block.gpsimd
        def _(e):
            for fn in ops["g"]:
                fn(e)

        @block.sync
        def _(e):
            for fn in ops["y"]:
                fn(e)

        @block.tensor
        def _(e):
            for fn in ops["t"]:
                fn(e)

        @block.vector
        def _(e):
            for fn in ops["v"]:
                fn(e)

        @block.scalar
        def _(e):
            for fn in ops["s"]:
                fn(e)

    return nc


def kernel(**inputs):
    A = np.ascontiguousarray
    x = np.asarray(inputs["L0_attn_x"], dtype=np.float32)
    shared = {
        "wqT": A(np.asarray(inputs["L0_attn_Wq"], np.float32).T),
        "wkT": A(np.asarray(inputs["L0_attn_Wk"], np.float32).T),
        "wvT": A(np.asarray(inputs["L0_attn_Wv"], np.float32).T),
        "woT": A(np.asarray(inputs["L0_attn_Wo"], np.float32).T),
        "bq": A(np.asarray(inputs["L0_attn_bq"], np.float32).reshape(DC, 128).T),
        "bk": A(np.asarray(inputs["L0_attn_bk"], np.float32).reshape(DC, 128).T),
        "bv": A(np.asarray(inputs["L0_attn_bv"], np.float32)),
        "bo": A(np.asarray(inputs["L0_attn_bo"], np.float32)),
        "gwT": A(np.asarray(inputs["L0_ffn_gate_weight"], np.float32).T),
        "eb": A(np.asarray(inputs["L0_ffn_e_bias"], np.float32)),
        "gpT": A(np.asarray(inputs["L0_ffn_gate_proj"], np.float32).transpose(0, 2, 1)),
        "upT": A(np.asarray(inputs["L0_ffn_up_proj"], np.float32).transpose(0, 2, 1)),
        "dpT": A(np.asarray(inputs["L0_ffn_down_proj"], np.float32).transpose(0, 2, 1)),
    }
    nc = build()
    in_maps = [dict(x=A(x[b]), **shared) for b in range(8)]
    res = run_bass_kernel_spmd(nc, in_maps, core_ids=list(range(8)))
    out = np.stack([np.asarray(r["out"]) for r in res.results], axis=0).astype(np.float32)
    return out
